# version 01 (final)
# baseline (speedup 1.0000x reference)
import numpy as np
import concourse.bass as bass
import concourse.mybir as mybir
from concourse.bass_utils import run_bass_kernel_spmd

F32 = mybir.dt.float32
BF16 = mybir.dt.bfloat16
I32 = mybir.dt.int32
AF = mybir.ActivationFunctionType
ALU = mybir.AluOpType

DEBUG = False
PHASE = 99
SUB = 99
NPAIR = None
WOUT = True
NOPD = 0
TMB = 9
NTILE, TTOK = 8, 512
NCORE, D = 8, 1024
HV, HK, DK, DV = 16, 8, 64, 64
MBNEG = -1e30

ENGINES = ("sync", "gpsimd", "vector", "scalar", "tensor")


class Op:
    __slots__ = ("eng", "fn", "deps", "dma", "idx", "seq", "inc_val", "needed", "inc_override", "hard", "semkey")

    def __init__(self, eng, fn, deps, dma, inc_override=None):
        self.eng, self.fn, self.deps, self.dma = eng, fn, deps, dma
        self.inc_override = inc_override
        self.hard = []
        self.idx = self.seq = self.inc_val = None
        self.semkey = None
        self.needed = False


class Prog:
    def __init__(self):
        self.ops = []
        self.lastop = {e: None for e in ENGINES}

    def emit(self, eng, fn, deps=(), dma=False, inc=None, hard=(), drain=False):
        deps = [d for d in deps if d is not None]
        op = Op(eng, fn, deps, dma, inc)
        op.hard = [d for d in hard if d is not None]
        if drain and self.lastop[eng] is not None:
            op.hard.append(self.lastop[eng])
        op.idx = len(self.ops)
        self.ops.append(op)
        self.lastop[eng] = op
        return op

    def barrier(self):
        return [self.lastop[e] for e in ENGINES if self.lastop[e] is not None]

    def finalize(self, block, sem_handles, final_deps=()):
        for op in self.ops:
            for d in op.deps:
                d.needed = True
            for d in op.hard:
                d.needed = True
        for d in final_deps:
            d.needed = True
        counts = {e: 0 for e in ENGINES}
        dma_lane_counts = {e: [0] * NDMALANES for e in ENGINES}
        dma_idx = {e: 0 for e in ENGINES}
        for op in self.ops:
            if op.dma:
                lane = dma_idx[op.eng] % NDMALANES
                dma_idx[op.eng] += 1
                op.semkey = (op.eng, lane)
                dma_lane_counts[op.eng][lane] += 16
                op.seq = dma_lane_counts[op.eng][lane]
                op.inc_val = 16
            elif op.needed:
                op.semkey = (op.eng, -1)
                op.inc_val = op.inc_override if op.inc_override is not None else 1
                counts[op.eng] += op.inc_val
                op.seq = counts[op.eng]
        by_eng = {e: [o for o in self.ops if o.eng == e] for e in ENGINES}

        def sem_of(key):
            eng, lane = key
            return sem_handles[eng] if lane < 0 else sem_handles[(eng, lane)]

        def make_body(eng_name):
            def body(eng):
                waited = {}
                for op in by_eng[eng_name]:
                    for d in list(op.deps) + list(op.hard):
                        if d.seq is None:
                            continue
                        key = d.semkey
                        if waited.get(key, -1) < d.seq:
                            eng.wait_ge(sem_of(key), d.seq)
                            waited[key] = d.seq
                    if op.dma and op.seq > 16:
                        key = op.semkey
                        if waited.get(key, -1) < op.seq - 16:
                            eng.wait_ge(sem_of(key), op.seq - 16)
                            waited[key] = op.seq - 16
                    ins = op.fn()
                    if op.inc_val is not None:
                        ins.then_inc(sem_of(op.semkey), op.inc_val)
                if eng_name == "sync":
                    w2 = {}
                    for d in final_deps:
                        key = d.semkey
                        if w2.get(key, -1) < d.seq:
                            eng.wait_ge(sem_of(key), d.seq)
                            w2[key] = d.seq
                    for e2 in ENGINES:
                        for lane in range(NDMALANES):
                            fin = dma_lane_counts[e2][lane]
                            if fin and w2.get((e2, lane), -1) < fin:
                                eng.wait_ge(sem_of((e2, lane)), fin)
                                w2[(e2, lane)] = fin
                        if counts[e2] and w2.get((e2, -1), -1) < counts[e2]:
                            eng.wait_ge(sem_of((e2, -1)), counts[e2])
                            w2[(e2, -1)] = counts[e2]
            return body

        block.sync(make_body("sync"))
        block.gpsimd(make_body("gpsimd"))
        block.vector(make_body("vector"))
        block.scalar(make_body("scalar"))
        block.tensor(make_body("tensor"))
        return counts


NDMALANES = 4


def vh_map(vh):
    kh = vh // 2
    rh = kh % 2
    fc = kh // 2
    par = vh % 2
    vhg = fc * 2 + par
    return rh, vhg, fc, par


def make_consts():
    cst = np.zeros((128, 1728), dtype=np.float32)
    np.fill_diagonal(cst[:, 0:128], 1.0)
    cst[:, 128:256] = 1.0
    bd = np.zeros((128, 128), dtype=np.float32)
    bd[0:64, 0:64] = 1.0
    bd[64:128, 64:128] = 1.0
    cst[:, 256:384] = bd
    mbi = np.full((64, 64), MBNEG, dtype=np.float32)
    for j in range(64):
        mbi[j, j:] = 0.0
    cst[0:64, 384:448] = mbi
    cst[64:128, 384:448] = mbi
    nmbs = np.full((64, 64), 1e30, dtype=np.float32)
    for i in range(64):
        nmbs[i, :i] = 0.0
    cst[0:64, 448:512] = nmbs
    cst[64:128, 448:512] = nmbs
    s01 = np.zeros((64, 64), dtype=np.float32)
    for j in range(64):
        s01[j, j + 1:] = 1.0
    cst[0:64, 512:576] = s01
    cst[64:128, 512:576] = s01
    cst[:, 576:640] = -1.0
    i64 = np.zeros((64, 64), dtype=np.float32)
    np.fill_diagonal(i64, 1.0)
    cst[0:64, 640:704] = i64
    cst[64:128, 640:704] = i64
    cst[:, 704:1216] = np.tile(cst[:, 384:448], (1, 8))
    cst[:, 1216:1728] = np.tile(cst[:, 448:512], (1, 8))
    return cst


def build_program():
    nc = bass.Bass("TRN2", target_bir_lowering=False)

    def MM(out, lhsT, rhs, **kw):
        if out.base_partition() > 0:
            kw["skip_group_check"] = True
        return nc.tensor.matmul(out, lhsT, rhs, **kw)
    p = Prog()

    xw = nc.dram_tensor("xw", [D, 4096], F32, kind="ExternalInput").ap()
    wqkv = nc.dram_tensor("wqkv", [128, 8, 2048], F32, kind="ExternalInput").ap()
    wz = nc.dram_tensor("wz", [128, 8, 1024], F32, kind="ExternalInput").ap()
    wba = nc.dram_tensor("wba", [128, 8, 64], F32, kind="ExternalInput").ap()
    wout = nc.dram_tensor("wout", [128, 8, 1024], F32, kind="ExternalInput").ap()
    convw = nc.dram_tensor("convw", [128, 16, 4], F32, kind="ExternalInput").ap()
    ppack = nc.dram_tensor("ppack", [128, 4], F32, kind="ExternalInput").ap()
    cstd = nc.dram_tensor("cstd", [128, 1728], F32, kind="ExternalInput").ap()
    wffn = nc.dram_tensor("wffn", [128, 8, 8, 1024], F32, kind="ExternalInput").ap()
    bffn = nc.dram_tensor("bffn", [128, 5, 8], F32, kind="ExternalInput").ap()
    coreoff = nc.dram_tensor("coreoff", [1, 1], I32, kind="ExternalInput").ap()

    smallfm_d = nc.dram_tensor("smallfm_d", [3, 16, 4096], F32, kind="Internal").ap()
    gcct_d = nc.dram_tensor("gcct_d", [64, 16], F32, kind="Internal").ap()
    ln2_d = nc.dram_tensor("ln2_d", [8, 8, 128, 512], BF16, kind="Internal").ap()
    ag_in = nc.dram_tensor("ag_in", [D, 4096], F32, kind="Internal").ap()
    ag_out = nc.dram_tensor("ag_out", [NCORE * D, 4096], F32, kind="Internal",
                            addr_space="Shared").ap()
    out_fm = nc.dram_tensor("out_fm", [8, 8, 128, 512], F32, kind="ExternalOutput").ap()

    dbg = {}
    if DEBUG:
        def dbg_t(name, shape, dt=F32):
            dbg[name] = nc.dram_tensor("dbg_" + name, shape, dt, kind="ExternalOutput").ap()
        dbg_t("ln1", [NTILE, 128, 4096])
        dbg_t("xt", [NTILE, 128, 4096])
        dbg_t("stat", [NTILE, 128, 1536])
        dbg_t("cstb", [1, 128, 384])
        dbg_t("mixed", [NTILE, 128, 16 * 515])
        dbg_t("qk", [NTILE, 128, 4096])
        dbg_t("v", [NTILE, 128, 4096])
        dbg_t("z", [NTILE, 128, 4096])
        dbg_t("sm1", [NTILE, 128, 2048])
        dbg_t("ogated", [NTILE, 128, 4096])
        dbg_t("xmid", [NTILE, 128, 4096])
        dbg_t("S", [64, 128, 512])
        dbg_t("TU", [32, 128, 2048])
        dbg_t("Bt", [32, 128, 1024])
        dbg_t("Rhat", [32, 128, 2048])
        dbg_t("mtilT", [32, 128, 1024])
        dbg_t("Mts", [32, 128, 1024])
        dbg_t("okt", [32, 128, 1024])

    SB_BASE = 17408
    SB_LIMIT = 229300
    ptr = [SB_BASE]
    sboff = {}

    def sb(name, shape, dtype, at=None):
        nbytes = int(np.prod(shape[1:])) * (2 if dtype == BF16 else 4)
        if at is None:
            at = ptr[0]
            ptr[0] = at + ((nbytes + 31) // 32) * 32
        sboff[name] = at
        return nc.alloc_sbuf_tensor_at(name, shape, dtype, offset=at).ap()

    cst = sb("cst", [128, 1728], F32)
    cstb = sb("cstb", [128, 384], BF16)
    cpool = sb("cpool", [128, 16], F32)
    pp = sb("pp", [128, 8], F32)
    convw_s = sb("convw_s", [128, 64], F32)
    soff = sb("soff", [1, 1], I32)
    Wqkv = sb("Wqkv", [128, 16384], BF16)
    Wz = sb("Wz", [128, 8192], BF16)
    Wba = sb("Wba", [128, 512], BF16)
    Wout = sb("Wout", [128, 8192], BF16)
    attn_w_end = ptr[0]

    xtile = sb("xtile", [128, 4096], F32)
    mixed = sb("mixed", [128, 16 * 515], BF16)
    zsilu = sb("zsilu", [128, 4096], BF16)
    vsilu = sb("vsilu", [128, 4096], BF16)
    qkf = sb("qkf", [128, 4096], BF16)
    ln1 = sb("ln1", [128, 4096], BF16)
    ogat = sb("ogat", [128, 4096], BF16, at=sboff["ln1"])
    sm1 = sb("sm1", [128, 2048], F32)
    stat = sb("stat", [128, 1536], F32)
    convt = sb("convt", [128, 2048], F32)
    rows = sb("rows", [1, 2080], F32)
    d2s = sb("d2s", [128, 1024], F32)
    es = sb("es", [128, 1024], F32)
    esp = sb("esp", [128, 1024], F32, at=sboff["convt"] + 4096)
    bmsk = sb("bmsk", [128, 1024], F32, at=sboff["convt"])
    mtilT = sb("mtilT", [128, 1024], BF16)
    Bt0 = sb("Bt0", [128, 1024], BF16)
    Bn0 = sb("Bn0", [128, 1024], BF16)
    BtP = [sb("BtPa", [128, 1024], BF16), sb("BtPb", [128, 1024], BF16, at=sboff["Bt0"])]
    BnP = [sb("BnPa", [128, 1024], BF16), sb("BnPb", [128, 1024], BF16, at=sboff["Bn0"])]
    Rm = [sb("Rma", [128, 2048], BF16), sb("Rmb", [128, 2048], BF16)]
    aob = sb("aob", [128, 4096], BF16, at=sboff["Rma"])
    Khat = sb("Khat", [128, 1024], BF16)
    Rhat = sb("Rhat", [128, 2048], BF16, at=sboff["Rmb"])
    OKTn = sb("OKTn", [128, 1024], BF16)
    Mts = sb("Mts", [128, 1024], BF16)
    dgc = sb("dgc", [128, 512], BF16)
    tmA = sb("tmA", [128, 32], F32)
    tmB = sb("tmB", [128, 112], F32)
    Ssb = sb("Ssb", [128, 512], BF16)
    Gevac = sb("Gevac", [128, 512], F32, at=sboff["stat"])
    ep1 = sb("ep1", [128, 512], F32)
    ep2 = sb("ep2", [128, 512], F32)
    o2s = sb("o2s", [128, 512], BF16)
    rinvs = sb("rinvs", [128, 512], F32, at=sboff["stat"] + 2048)
    attn_end = ptr[0]
    assert attn_end <= SB_LIMIT, f"attn SBUF overflow: {attn_end}"

    fptr = [sboff["Wqkv"]]

    def sbf(name, shape, dtype):
        nbytes = int(np.prod(shape[1:])) * (2 if dtype == BF16 else 4)
        at = fptr[0]
        fptr[0] = at + ((nbytes + 31) // 32) * 32
        sboff[name] = at
        return nc.alloc_sbuf_tensor_at(name, shape, dtype, offset=at).ap()

    Wffn = sbf("Wffn", [128, 8 * 8192], BF16)
    bffn_s = sbf("bffn_s", [128, 40], F32)
    cstate = sbf("cstate", [128, 4096], F32)
    cbf = sbf("cbf", [128, 4096], BF16)
    xmt = sbf("xmt", [128, 4096], F32)
    ln2t = sbf("ln2t", [128, 4096], BF16)
    fgi = sbf("fgi", [128, 1536], F32)
    hbt = sbf("hbt", [128, 4096], BF16)
    statf = sbf("statf", [128, 1536], F32)
    outt = sb("outt", [128, 1024], F32, at=sboff["fgi"])
    hsq = sbf("hsq", [128, 512], BF16)
    assert fptr[0] <= SB_LIMIT, f"ffn SBUF overflow: {fptr[0]}"

    P = nc.alloc_psum_tensor("P", [128, 3584], F32).ap()
    Pb = nc.alloc_psum_tensor("Pb", [128, 1024], BF16).ap()

    CPOOL = {}
    cpool_ops = []

    def cp(v):
        v = float(v)
        if v not in CPOOL:
            idx = len(CPOOL)
            assert idx < 16
            op = p.emit("vector", lambda idx=idx, v=v: nc.vector.memset(
                cpool[:, idx:idx + 1], v))
            cpool_ops.append(op)
            CPOOL[v] = idx
        i = CPOOL[v]
        return cpool[:, i:i + 1]

    I128f = cst[:, 0:128]
    ONES = cst[:, 128:256]
    BD64 = cst[:, 256:384]
    MBI2 = cst[:, 384:448]
    NMBS2 = cst[:, 448:512]
    S01 = cst[:, 512:576]
    NEG1 = cst[:, 576:640]
    IST2 = cst[:, 640:704]
    MBI2R = cst[:, 704:1216]
    NMBSR = cst[:, 1216:1728]
    I128b = cstb[:, 0:128]
    ONESb = cstb[:, 128:256]
    BD64b = cstb[:, 256:384]

    def tc(a, n=512):
        return slice(a, a + n)

    for _v in (0.0, 1.0, 1e-5, 1e-6, float(np.log(DK ** -0.5))):
        cp(_v)

    ld_cst = p.emit("sync", lambda: nc.sync.dma_start(out=cst[:, :], in_=cstd), dma=True)
    ld_cstb = p.emit("gpsimd", lambda: nc.gpsimd.dma_start(out=cstb[:, :], in_=cstd[:, 0:384]), dma=True)
    ld_pp = p.emit("sync", lambda: nc.sync.dma_start(out=pp[:, 0:4], in_=ppack), dma=True)
    ld_cw = p.emit("sync", lambda: nc.sync.dma_start(
        out=convw_s[:].rearrange("q (a b) -> q a b", a=16), in_=convw), dma=True)
    ld_off = p.emit("sync", lambda: nc.sync.dma_start(out=soff[:, :], in_=coreoff), dma=True)
    wl_qkv = p.emit("gpsimd", lambda: nc.gpsimd.dma_start(
        out=Wqkv[:].rearrange("q (a b) -> q a b", a=8), in_=wqkv), dma=True)
    wl_z = p.emit("gpsimd", lambda: nc.gpsimd.dma_start(
        out=Wz[:].rearrange("q (a b) -> q a b", a=8), in_=wz), dma=True)
    wl_ba = p.emit("gpsimd", lambda: nc.gpsimd.dma_start(
        out=Wba[:].rearrange("q (a b) -> q a b", a=8), in_=wba), dma=True)
    wl_out = p.emit("gpsimd", lambda: nc.gpsimd.dma_start(
        out=Wout[:].rearrange("q (a b) -> q a b", a=8), in_=wout), dma=True)
    e1 = p.emit("scalar", lambda: nc.scalar.activation(pp[:, 4:5], pp[:, 1:2], AF.Exp), deps=[ld_pp])
    e2 = p.emit("vector", lambda: nc.vector.tensor_scalar_mul(pp[:, 3:4], pp[:, 4:5], -1.0), deps=[e1])
    mz = p.emit("vector", lambda: nc.vector.memset(
        mixed[:].rearrange("q (a b) -> q a b", a=16)[:, :, 0:3], 0.0))
    zcar = p.emit("vector", lambda: nc.vector.memset(sm1[:, :], 0.0))
    zrows = p.emit("gpsimd", lambda: nc.gpsimd.memset(rows[:, :], 0.0))
    ztm = p.emit("vector", lambda: nc.vector.memset(tmB[:, :], 0.0), deps=[zcar])
    szero = p.emit("vector", lambda: nc.vector.memset(Ssb[:, :], 0.0))
    zmark = p.emit("vector", lambda: nc.vector.memset(cpool[:, 14:16], 0.0))

    ag_writes = []
    prev_tile_done = [ld_cst, ld_cstb, ld_cw, e2, mz, zcar, zrows, ztm, szero, zmark, wl_qkv, wl_z, wl_ba, wl_out, ld_off] + cpool_ops
    prev_hist = [mz]
    prev_carry = [zcar]
    prev_state = [szero]

    ntile_eff = NTILE if PHASE >= 4 else 1
    for tau in range(ntile_eff):
        tile_start_deps = list(prev_tile_done)
        ldx = p.emit("sync", lambda tau=tau: nc.sync.dma_start(
            out=xtile[:].rearrange("q (dc t) -> q dc t", dc=8),
            in_=xw.rearrange("(dc q) t -> q dc t", q=128)[:, :, tc(tau * TTOK)]),
            dma=True, deps=tile_start_deps)

        xbf = ln1
        casts = [p.emit("vector", lambda dc=dc: nc.vector.tensor_copy(
            xbf[:, tc(dc * 512)], xtile[:, tc(dc * 512)]), deps=[ldx] + tile_start_deps)
            for dc in range(8)]
        sqs = [p.emit("scalar", lambda dc=dc: nc.scalar.activation(
            vsilu[:, tc(dc * 512)], xbf[:, tc(dc * 512)], AF.Square),
            deps=[casts[dc]] + tile_start_deps) for dc in range(8)]
        mm_s1 = []
        for dc in range(8):
            mm_s1.append(p.emit("tensor", lambda dc=dc: MM(
                P[:, 0:512], ONESb, xbf[:, tc(dc * 512)], start=(dc == 0), stop=(dc == 7)),
                deps=[casts[dc]] + tile_start_deps))
        mm_s2 = []
        for dc in range(8):
            mm_s2.append(p.emit("tensor", lambda dc=dc: MM(
                P[:, 512:1024], ONESb, vsilu[:, tc(dc * 512)], start=(dc == 0), stop=(dc == 7)),
                deps=[sqs[dc]]))
        mu = p.emit("vector", lambda: nc.vector.tensor_scalar_mul(
            stat[:, 0:512], P[:, 0:512], 1.0 / 1024.0), deps=[mm_s1[-1]])
        mu2 = p.emit("scalar", lambda: nc.scalar.activation(
            stat[:, 1024:1536], stat[:, 0:512], AF.Square), deps=[mu])
        s2m = p.emit("vector", lambda: nc.vector.tensor_scalar_mul(
            stat[:, 512:1024], P[:, 512:1024], 1.0 / 1024.0), deps=[mm_s2[-1]])
        var = p.emit("vector", lambda: nc.vector.tensor_sub(
            stat[:, 512:1024], stat[:, 512:1024], stat[:, 1024:1536]), deps=[s2m, mu2])
        lnv = p.emit("scalar", lambda: nc.scalar.activation(
            stat[:, 1024:1536], stat[:, 512:1024], AF.Ln, bias=cp(1e-5)), deps=[var])
        rinv = p.emit("scalar", lambda: nc.scalar.activation(
            stat[:, 512:1024], stat[:, 1024:1536], AF.Exp, scale=-0.5), deps=[lnv])
        ln1ops = []
        for dc in range(8):
            a_ = p.emit("vector", lambda dc=dc: nc.vector.tensor_sub(
                convt[:, 0:512], xtile[:, tc(dc * 512)], stat[:, 0:512]),
                deps=[mu, rinv, prev_hist[0]] + ([ln1ops[-1]] if ln1ops else []))
            b_ = p.emit("vector", lambda dc=dc: nc.vector.tensor_mul(
                ln1[:, tc(dc * 512)], convt[:, 0:512], stat[:, 512:1024]), deps=[a_])
            ln1ops.append(b_)
        if DEBUG:
            p.emit("gpsimd", lambda tau=tau: nc.gpsimd.dma_start(
                out=dbg["ln1"][tau], in_=ln1[:, :]), dma=True, deps=ln1ops)
            p.emit("sync", lambda tau=tau: nc.sync.dma_start(
                out=dbg["xt"][tau], in_=xtile[:, :]), dma=True, deps=[ldx] + ln1ops)
            p.emit("sync", lambda tau=tau: nc.sync.dma_start(
                out=dbg["stat"][tau], in_=stat[:, :]), dma=True, deps=[rinv, mu, var] + ln1ops)
            if tau == 0:
                p.emit("gpsimd", lambda: nc.gpsimd.dma_start(
                    out=dbg["cstb"][0], in_=cstb[:, :]), dma=True, deps=[ld_cstb] + ln1ops)

        mixed3 = mixed[:].rearrange("q (mc t) -> q mc t", mc=16)
        proj_done = []
        prev_ev = None
        for mc in range(16):
            mms = []
            for kc in range(8):
                mms.append(p.emit("tensor", lambda mc=mc, kc=kc: MM(
                    P[:, 1024:1536],
                    Wqkv[:].rearrange("q (kc m) -> q kc m", kc=8)[:, kc, tc(mc * 128, 128)],
                    ln1[:, tc(kc * 512)], start=(kc == 0), stop=(kc == 7)),
                    deps=[ln1ops[kc], prev_ev]))
            eng = "scalar" if mc % 2 == 0 else "vector"
            fnc = (lambda mc=mc: nc.scalar.copy(mixed3[:, mc, 3:515], P[:, 1024:1536])) \
                if mc % 2 == 0 else \
                (lambda mc=mc: nc.vector.tensor_copy(mixed3[:, mc, 3:515], P[:, 1024:1536]))
            prev_ev = p.emit(eng, fnc, deps=[mms[-1], prev_hist[0]])
            proj_done.append(prev_ev)
        zevs = []
        prev_zev = None
        for mc in range(8):
            mms = []
            for kc in range(8):
                mms.append(p.emit("tensor", lambda mc=mc, kc=kc: MM(
                    P[:, 1536:2048],
                    Wz[:].rearrange("q (kc m) -> q kc m", kc=8)[:, kc, tc(mc * 128, 128)],
                    ln1[:, tc(kc * 512)], start=(kc == 0), stop=(kc == 7)),
                    deps=[ln1ops[kc], prev_zev]))
            prev_zev = p.emit("scalar", lambda mc=mc: nc.scalar.activation(
                zsilu[:, tc(mc * 512)], P[:, 1536:2048], AF.Silu), deps=[mms[-1]])
            zevs.append(prev_zev)
        mm_ba = []
        for kc in range(8):
            mm_ba.append(p.emit("tensor", lambda kc=kc: MM(
                P[0:64, 2048:2560],
                Wba[:].rearrange("q (kc m) -> q kc m", kc=8)[:, kc, :],
                ln1[:, tc(kc * 512)], start=(kc == 0), stop=(kc == 7)),
                deps=[ln1ops[kc]]))
        bsig = p.emit("scalar", lambda: nc.scalar.activation(
            sm1[0:16, 0:512], P[0:16, 2048:2560], AF.Sigmoid), deps=[mm_ba[-1], prev_carry[0]])
        gabs = p.emit("scalar", lambda: nc.scalar.activation(
            sm1[32:48, 0:512], P[32:48, 2048:2560], AF.Abs, bias=pp[32:48, 2:3]),
            deps=[mm_ba[-1], prev_carry[0], ld_pp])
        gexp = p.emit("scalar", lambda: nc.scalar.activation(
            sm1[32:48, 512:1024], sm1[32:48, 0:512], AF.Exp, scale=-1.0), deps=[gabs])
        gln = p.emit("scalar", lambda: nc.scalar.activation(
            sm1[32:48, 512:1024], sm1[32:48, 512:1024], AF.Ln, bias=cp(1.0)[32:48, :]), deps=[gexp])
        grelu = p.emit("scalar", lambda: nc.scalar.activation(
            sm1[32:48, 0:512], P[32:48, 2048:2560], AF.Relu, bias=pp[32:48, 2:3]), deps=[gabs])
        gsp = p.emit("vector", lambda: nc.vector.tensor_add(
            sm1[32:48, 0:512], sm1[32:48, 0:512], sm1[32:48, 512:1024]), deps=[gln, grelu])
        gfin = p.emit("vector", lambda: nc.vector.tensor_scalar_mul(
            sm1[32:48, 512:1024], sm1[32:48, 0:512], pp[32:48, 3:4]), deps=[gsp, e2])
        scan = p.emit("vector", lambda: nc.vector.tensor_tensor_scan(
            sm1[32:48, 1024:1536], sm1[32:48, 512:1024],
            cp(0.0)[32:48, :].broadcast_to([16, 512]),
            sm1[32:48, 1560:1561], ALU.add, ALU.add), deps=[gfin])
        ends = p.emit("vector", lambda: nc.vector.tensor_copy(
            sm1[32:48, 1536:1544],
            sm1[32:48, 1024:1536].rearrange("q (c j) -> q c j", c=8)[:, :, 63]), deps=[scan])
        bprev = p.emit("vector", lambda: nc.vector.tensor_copy(
            sm1[32:48, 1544:1545], sm1[32:48, 1560:1561]), deps=[scan])
        brest = p.emit("vector", lambda: nc.vector.tensor_copy(
            sm1[32:48, 1545:1552], sm1[32:48, 1536:1543]), deps=[ends])
        ncar = p.emit("vector", lambda: nc.vector.tensor_copy(
            sm1[32:48, 1560:1561], sm1[32:48, 1543:1544]), deps=[ends, bprev])
        prev_carry[0] = ncar
        gcr = p.emit("vector", lambda: nc.vector.tensor_sub(
            sm1[32:48, 1024:1536].rearrange("q (c j) -> q c j", c=8),
            sm1[32:48, 1024:1536].rearrange("q (c j) -> q c j", c=8),
            sm1[32:48, 1544:1552][:, :, None].broadcast_to([16, 8, 64])),
            deps=[bprev, brest])
        gam = p.emit("scalar", lambda: nc.scalar.activation(
            sm1[32:48, 512:1024], sm1[32:48, 1024:1536], AF.Exp), deps=[gcr])
        gcc = p.emit("vector", lambda: nc.vector.tensor_sub(
            sm1[32:48, 1552:1560], sm1[32:48, 1536:1544], sm1[32:48, 1544:1552]),
            deps=[ends, bprev, brest])
        dma_g = p.emit("sync", lambda tau=tau: nc.sync.dma_start(
            out=smallfm_d[0, :, tc(tau * TTOK)], in_=sm1[32:48, 1024:1536]), dma=True, deps=[gcr])
        dma_gm = p.emit("sync", lambda tau=tau: nc.sync.dma_start(
            out=smallfm_d[1, :, tc(tau * TTOK)], in_=sm1[32:48, 512:1024]), dma=True, deps=[gam])
        dma_b = p.emit("sync", lambda tau=tau: nc.sync.dma_start(
            out=smallfm_d[2, :, tc(tau * TTOK)], in_=sm1[0:16, 0:512]), dma=True, deps=[bsig])
        tr_gcc = p.emit("tensor", lambda: nc.tensor.transpose(
            P[0:8, 3120:3136], sm1[32:48, 1552:1560], I128f[32:48, 32:48]), deps=[gcc, ld_cst], drain=True)
        ev_gcc = p.emit("vector", lambda: nc.vector.tensor_copy(
            sm1[0:8, 1984:2000], P[0:8, 3120:3136]), deps=[tr_gcc])
        dma_gcc = p.emit("sync", lambda tau=tau: nc.sync.dma_start(
            out=gcct_d[tau * 8:(tau + 1) * 8, :], in_=sm1[0:8, 1984:2000]), dma=True, deps=[ev_gcc])
        if DEBUG:
            p.emit("sync", lambda tau=tau: nc.sync.dma_start(
                out=dbg["sm1"][tau], in_=sm1[:, :]), dma=True, deps=[gam, gcc, bsig, dma_g])

        qkv_done = []
        conv_tail = []
        prev_branch = None
        prev_lns = [None]
        for mc in range(16):
            t0c = p.emit("scalar", lambda mc=mc: nc.scalar.mul(
                convt[:, 0:512], mixed3[:, mc, 0:512], convw_s[:, mc * 4:mc * 4 + 1]),
                deps=[proj_done[mc], ld_cw, prev_branch])
            t1c = p.emit("vector", lambda mc=mc: nc.vector.scalar_tensor_tensor(
                convt[:, 512:1024], mixed3[:, mc, 1:513], convw_s[:, mc * 4 + 1:mc * 4 + 2],
                convt[:, 0:512], ALU.mult, ALU.add), deps=[t0c, prev_branch])
            t2c = p.emit("vector", lambda mc=mc: nc.vector.scalar_tensor_tensor(
                convt[:, 1024:1536], mixed3[:, mc, 2:514], convw_s[:, mc * 4 + 2:mc * 4 + 3],
                convt[:, 512:1024], ALU.mult, ALU.add), deps=[t1c, prev_branch])
            t3c = p.emit("vector", lambda mc=mc: nc.vector.scalar_tensor_tensor(
                convt[:, 1536:2048], mixed3[:, mc, 3:515], convw_s[:, mc * 4 + 3:mc * 4 + 4],
                convt[:, 1024:1536], ALU.mult, ALU.add), deps=[t2c, prev_branch])
            conv_tail.append(t3c)
            if mc >= 8:
                sv = p.emit("scalar", lambda mc=mc: nc.scalar.activation(
                    vsilu[:, tc((mc - 8) * 512)], convt[:, 1536:2048], AF.Silu),
                    deps=[t3c, mm_s2[-1]])
                qkv_done.append(sv)
                prev_branch = sv
            else:
                sq_ = p.emit("scalar", lambda: nc.scalar.activation(
                    convt[:, 0:512], convt[:, 1536:2048], AF.Silu), deps=[t3c])
                ss = p.emit("scalar", lambda: nc.scalar.activation(
                    convt[:, 512:1024], convt[:, 0:512], AF.Square), deps=[sq_])
                smm = p.emit("tensor", lambda: MM(
                    P[:, 2560:3072], BD64, convt[:, 512:1024], start=True, stop=True),
                    deps=[ss, mm_ba[-1], prev_lns[0]])
                bias = cp(float(np.log(DK ** -0.5))) if mc < 4 else cp(0.0)
                lns = p.emit("scalar", lambda: nc.scalar.activation(
                    convt[:, 1024:1536], P[:, 2560:3072], AF.Ln, bias=cp(1e-6)), deps=[smm, t2c])
                prev_lns[0] = lns
                exr = p.emit("scalar", lambda bias=bias: nc.scalar.activation(
                    convt[:, 512:1024], convt[:, 1024:1536], AF.Exp, scale=-0.5, bias=bias),
                    deps=[lns])
                fin = p.emit("vector", lambda mc=mc: nc.vector.tensor_mul(
                    qkf[:, tc(mc * 512)], convt[:, 0:512], convt[:, 512:1024]), deps=[exr])
                qkv_done.append(fin)
                prev_branch = fin
        dmx = None
        if DEBUG:
            dmx = p.emit("gpsimd", lambda tau=tau: nc.gpsimd.dma_start(
                out=dbg["mixed"][tau].rearrange("q (a b) -> q a b", a=16), in_=mixed3[:, :, :]),
                dma=True, deps=proj_done + conv_tail)
        hist = p.emit("vector", lambda: nc.vector.tensor_copy(
            mixed3[:, :, 0:3], mixed3[:, :, 512:515]), deps=conv_tail + [dmx])
        prev_hist[0] = hist
        if DEBUG:
            p.emit("gpsimd", lambda tau=tau: nc.gpsimd.dma_start(
                out=dbg["qk"][tau], in_=qkf[:, :]), dma=True, deps=qkv_done)
            p.emit("gpsimd", lambda tau=tau: nc.gpsimd.dma_start(
                out=dbg["v"][tau], in_=vsilu[:, :]), dma=True, deps=qkv_done)
            p.emit("gpsimd", lambda tau=tau: nc.gpsimd.dma_start(
                out=dbg["z"][tau], in_=zsilu[:, :]), dma=True, deps=zevs)

        qsl = qkf[:].rearrange("q (a t) -> q a t", a=8)
        vsl = vsilu[:].rearrange("q (a t) -> q a t", a=8)
        def akap(rh):
            return P[:, rh * 512: rh * 512 + 256].rearrange("q (khq i) -> q khq i", khq=4)

        def aqap(rh):
            return P[:, rh * 512 + 256: rh * 512 + 512].rearrange("q (khq i) -> q khq i", khq=4)
        prev_pair = [None]
        npair_eff = NPAIR if NPAIR is not None else (4 if PHASE >= 3 else (1 if PHASE >= 2 else 0))
        for u in range(npair_eff):
            cg0 = tau * 8 + 2 * u
            l0 = 2 * u * 64
            dep_in = qkv_done + [dma_g, dma_gm, dma_b, dma_gcc] + \
                ([prev_pair[0]] if prev_pair[0] is not None else [mu, rinv])

            ak_mms, aqt_mms = [], []
            for kh in range(8 if SUB >= 2 else 0):
                fc, rh = kh // 2, kh % 2
                for ch in range(2):
                    ak_mms.append(p.emit("tensor", lambda fc=fc, rh=rh, ch=ch, l0=l0, kh=kh:
                        MM(
                            akap(rh)[ch * 64:(ch + 1) * 64, kh // 2, :],
                            qsl[64 * rh:64 * (rh + 1), 4 + fc, l0 + ch * 64: l0 + (ch + 1) * 64],
                            qsl[64 * rh:64 * (rh + 1), 4 + fc, l0 + ch * 64: l0 + (ch + 1) * 64],
                            start=True, stop=True,
                            tile_position=(64 * rh, 64 * ch)),
                        deps=dep_in, drain=(kh == 0 and ch == 0)))
                    aqt_mms.append(p.emit("tensor", lambda fc=fc, rh=rh, ch=ch, l0=l0, kh=kh:
                        MM(
                            aqap(rh)[ch * 64:(ch + 1) * 64, kh // 2, :],
                            qsl[64 * rh:64 * (rh + 1), 4 + fc, l0 + ch * 64: l0 + (ch + 1) * 64],
                            qsl[64 * rh:64 * (rh + 1), 0 + fc, l0 + ch * 64: l0 + (ch + 1) * 64],
                            start=True, stop=True,
                            tile_position=(64 * rh, 64 * ch)),
                        deps=dep_in))

            ak_serial = ak_mms + aqt_mms
            r_g = r_gcc = None
            if NOPD in (0, 2):
                r_g = p.emit("scalar", lambda tau=tau, l0=l0: nc.scalar.dma_start(
                    out=rows[0:1, 0:2048].rearrange("q (c h i) -> q c h i", c=2, h=16),
                    in_=smallfm_d[0, :, tau * TTOK + l0: tau * TTOK + l0 + 128]
                    .rearrange("h (c i) -> c h i", c=2).unsqueeze(0)), dma=True,
                    deps=[dma_g, prev_pair[0]] + ak_serial)
            if NOPD in (0, 3):
                r_gcc = p.emit("sync", lambda cg0=cg0: nc.sync.dma_start(
                    out=rows[0:1, 2048:2080].rearrange("q (c h) -> q c h", c=2),
                    in_=gcct_d[cg0:cg0 + 2, :].unsqueeze(0)), dma=True,
                    deps=[dma_gcc, prev_pair[0]] + ak_serial)

            if SUB < 3:
                prev_pair[0] = p.emit("gpsimd", lambda: nc.gpsimd.tensor_copy(
                    cpool[:, 15:16], cpool[:, 15:16]),
                    deps=ak_mms + aqt_mms + [r_g, r_gcc])
                continue
            if TMB < 1:
                prev_pair[0] = p.emit("gpsimd", lambda: nc.gpsimd.tensor_copy(
                    cpool[:, 15:16], cpool[:, 15:16]),
                    deps=ak_mms + aqt_mms + [r_g, r_gcc])
                continue
            trb = p.emit("tensor", lambda l0=l0: nc.tensor.transpose(
                P[:, 3072:3088], sm1[0:16, l0:l0 + 128], I128f[0:16, 0:16]), deps=dep_in,
                drain=True)
            if TMB < 2:
                prev_pair[0] = p.emit("gpsimd", lambda: nc.gpsimd.tensor_copy(
                    cpool[:, 15:16], cpool[:, 15:16]),
                    deps=ak_mms + aqt_mms + [r_g, r_gcc, trb])
                continue
            trg = p.emit("tensor", lambda l0=l0: nc.tensor.transpose(
                P[:, 2560:2576], sm1[32:48, 1024 + l0:1152 + l0], I128f[32:48, 32:48]),
                deps=dep_in + [gcr])
            ev_b = p.emit("vector", lambda: nc.vector.tensor_copy(tmA[:, 0:16], P[:, 3072:3088]),
                          deps=[trb])
            ev_g = p.emit("vector", lambda: nc.vector.tensor_copy(tmA[:, 16:32], P[:, 2560:2576]),
                          deps=[trg])
            gamTM = p.emit("scalar", lambda: nc.scalar.activation(
                tmB[:, 0:16], P[:, 2560:2576], AF.Exp), deps=[trg])
            bgn = p.emit("vector", lambda: nc.vector.scalar_tensor_tensor(
                tmB[:, 16:32], tmB[:, 0:16], -1.0, tmA[:, 0:16], ALU.mult, ALU.mult),
                deps=[gamTM, ev_b])
            bneg = p.emit("vector", lambda: nc.vector.tensor_scalar_mul(
                tmB[:, 64:80], tmA[:, 0:16], -1.0), deps=[ev_b])
            if TMB < 3:
                prev_pair[0] = p.emit("gpsimd", lambda: nc.gpsimd.tensor_copy(
                    cpool[:, 15:16], cpool[:, 15:16]),
                    deps=ak_mms + aqt_mms + [r_g, r_gcc, ev_b, ev_g, gamTM, bgn, bneg])
                continue
            k1f = p.emit("tensor", lambda: MM(
                P[:, 2048:2080], ONES[0:1, 0:128], rows[0:1, 2048:2080],
                start=True, stop=True), deps=[r_gcc, ld_cst] + dep_in, drain=True)
            cog0_h = [p.emit("vector", lambda rh=rh: nc.vector.tensor_sub(
                tmB[rh * 64:(rh + 1) * 64, 48:64],
                P[rh * 64:(rh + 1) * 64, 2048 + 16 * rh:2064 + 16 * rh],
                tmA[rh * 64:(rh + 1) * 64, 16:32]), deps=[k1f, ev_g]) for rh in range(2)]
            cog = p.emit("scalar", lambda: nc.scalar.activation(
                tmB[:, 48:64], tmB[:, 48:64], AF.Exp), deps=cog0_h)
            if TMB < 4:
                prev_pair[0] = p.emit("gpsimd", lambda: nc.gpsimd.tensor_copy(
                    cpool[:, 15:16], cpool[:, 15:16]),
                    deps=ak_mms + aqt_mms + [r_g, r_gcc, ev_b, ev_g, gamTM, bgn, bneg, cog])
                continue
            gcg_evs = [p.emit("scalar", lambda: nc.scalar.activation(
                tmB[:, 80:112], P[:, 2048:2080], AF.Exp), deps=[k1f])] * 2

            if SUB < 4:
                prev_pair[0] = p.emit("gpsimd", lambda: nc.gpsimd.tensor_copy(
                    cpool[:, 15:16], cpool[:, 15:16]),
                    deps=[gamTM, cog, ev_b, ev_g, bgn, bneg] + gcg_evs + ak_mms + aqt_mms + [r_g, r_gcc])
                continue
            if TMB < 5:
                prev_pair[0] = p.emit("gpsimd", lambda: nc.gpsimd.tensor_copy(
                    cpool[:, 15:16], cpool[:, 15:16]),
                    deps=ak_mms + aqt_mms + [r_g, r_gcc, ev_b, ev_g, gamTM, bgn, bneg, cog] + gcg_evs)
                continue
            vt_mms = [p.emit("tensor", lambda fc=fc, l0=l0: MM(
                Pb[:, tc(fc * 128, 128)], vsl[:, fc, l0:l0 + 128], I128b,
                is_transpose=True, start=(fc == 0), stop=(fc == 7)),
                deps=dep_in + [ld_cstb], drain=(fc == 0)) for fc in range(8)]
            rdv = p.emit("vector", lambda: nc.vector.tensor_mul(
                Rhat[:].rearrange("q (h w) -> q h w", h=16)[:, :, 0:64],
                Pb[:].rearrange("q (h dv) -> q h dv", h=16),
                tmA[:, 0:16][:, :, None].broadcast_to([128, 16, 64])),
                deps=[ev_b] + vt_mms)
            kt_mms = [p.emit("tensor", lambda fc=fc, l0=l0: MM(
                Pb[:, tc(512 + fc * 128, 128)], qsl[:, 4 + fc, l0:l0 + 128], I128b,
                is_transpose=True, start=(fc == 0), stop=(fc == 3)),
                deps=[rdv] + dep_in) for fc in range(4)]
            rdk = p.emit("vector", lambda: nc.vector.tensor_mul(
                Rhat[:].rearrange("q (a b w) -> q a b w", a=8, b=2)[:, :, :, 64:128],
                Pb[:, 512:1024].rearrange("q (a d) -> q a d", a=8)[:, :, None, :]
                .broadcast_to([128, 8, 2, 64]),
                tmB[:, 16:32].rearrange("q (a b) -> q a b", a=8)[:, :, :, None]
                .broadcast_to([128, 8, 2, 64])),
                deps=[bgn] + kt_mms)
            khat_ev = p.emit("vector", lambda: nc.vector.tensor_mul(
                Khat[:].rearrange("q (a b d) -> q a b d", a=8, b=2),
                Pb[:, 512:1024].rearrange("q (a d) -> q a d", a=8)[:, :, None, :]
                .broadcast_to([128, 8, 2, 64]),
                tmB[:, 48:64].rearrange("q (a b) -> q a b", a=8)[:, :, :, None]
                .broadcast_to([128, 8, 2, 64])),
                deps=[cog] + kt_mms)

            if SUB < 5:
                prev_pair[0] = p.emit("gpsimd", lambda: nc.gpsimd.tensor_copy(
                    cpool[:, 15:16], cpool[:, 15:16]),
                    deps=[rdv, rdk, khat_ev, gamTM, cog, ev_b, ev_g, bgn, bneg] + gcg_evs + ak_mms + aqt_mms + [r_g, r_gcc])
                continue
            gr = []
            gr.append(p.emit("tensor", lambda: MM(
                P[0:64, 1024:1536], ONES[0:1, 0:64], rows[0:1, 0:512], start=True, stop=True,
                tile_position=(0, 0)), deps=[r_g] + dep_in, drain=True))
            gr.append(p.emit("tensor", lambda: MM(
                P[0:64, 1536:2048], ONES[0:1, 0:64], rows[0:1, 512:1024], start=True, stop=True,
                tile_position=(0, 0)), deps=[r_g] + dep_in))
            gr.append(p.emit("tensor", lambda: MM(
                P[64:128, 1024:1536], ONES[0:1, 0:64], rows[0:1, 1024:1536], start=True, stop=True,
                tile_position=(0, 64)), deps=[r_g]))
            gr.append(p.emit("tensor", lambda: MM(
                P[64:128, 1536:2048], ONES[0:1, 0:64], rows[0:1, 1536:2048], start=True, stop=True,
                tile_position=(0, 64)), deps=[r_g]))
            gr.append(p.emit("tensor", lambda: MM(
                P[:, 1024:1536], I128f, MBI2R,
                start=False, stop=True, skip_group_check=True), deps=gr[:4], hard=[gr[3]],
                drain=True))
            gr.append(p.emit("tensor", lambda: MM(
                P[:, 1536:2048], I128f, MBI2R,
                start=False, stop=True, skip_group_check=True), deps=gr[:4]))
            d2 = p.emit("vector", lambda: nc.vector.tensor_sub(
                d2s[:].rearrange("q (h i) -> q h i", h=16),
                P[:, 1024:2048].rearrange("q (h i) -> q h i", h=16),
                tmA[:, 16:32][:, :, None].broadcast_to([128, 16, 64])), deps=gr + [ev_g])
            e_i = p.emit("scalar", lambda: nc.scalar.activation(es[:, :], d2s[:, :], AF.Exp),
                         deps=[d2])
            mt_h = []
            for rh in range(2):
                mt_h.append(p.emit("vector", lambda rh=rh: nc.vector.tensor_mul(
                    mtilT[:].rearrange("q (khq rh2 par i) -> q khq rh2 par i", khq=4, rh2=2, par=2)
                    [:, :, rh, :, :],
                    aqap(rh)[:, :, None, :].broadcast_to([128, 4, 2, 64]),
                    es[:].rearrange("q (khq rh2 par i) -> q khq rh2 par i", khq=4, rh2=2, par=2)
                    [:, :, rh, :, :]), deps=[e_i] + aqt_mms))
            mt_ = p.emit("vector", lambda: nc.vector.tensor_copy(
                cpool[:, 14:15], cpool[:, 14:15]), deps=mt_h)
            sr = []
            sr.append(p.emit("tensor", lambda: MM(
                P[0:64, 1024:1536], ONES[0:1, 0:64], rows[0:1, 0:512], start=True, stop=True,
                tile_position=(0, 0)), deps=[r_g, d2], drain=True))
            sr.append(p.emit("tensor", lambda: MM(
                P[0:64, 1536:2048], ONES[0:1, 0:64], rows[0:1, 512:1024], start=True, stop=True,
                tile_position=(0, 0)), deps=[r_g, d2]))
            sr.append(p.emit("tensor", lambda: MM(
                P[64:128, 1024:1536], ONES[0:1, 0:64], rows[0:1, 1024:1536], start=True, stop=True,
                tile_position=(0, 64)), deps=[r_g, d2]))
            sr.append(p.emit("tensor", lambda: MM(
                P[64:128, 1536:2048], ONES[0:1, 0:64], rows[0:1, 1536:2048], start=True, stop=True,
                tile_position=(0, 64)), deps=[r_g, d2]))
            sr.append(p.emit("tensor", lambda: MM(
                P[:, 1024:1536], I128f, NMBSR,
                start=False, stop=True, skip_group_check=True), deps=sr[:4], hard=[sr[3]],
                drain=True))
            sr.append(p.emit("tensor", lambda: MM(
                P[:, 1536:2048], I128f, NMBSR,
                start=False, stop=True, skip_group_check=True), deps=sr[:4]))
            d2p = p.emit("vector", lambda: nc.vector.tensor_sub(
                d2s[:].rearrange("q (h i) -> q h i", h=16),
                tmA[:, 16:32][:, :, None].broadcast_to([128, 16, 64]),
                P[:, 1024:2048].rearrange("q (h i) -> q h i", h=16)), deps=sr + [ev_g, e_i])
            e_s = p.emit("scalar", lambda: nc.scalar.activation(esp[:, :], d2s[:, :], AF.Exp),
                         deps=[d2p])
            r_b = p.emit("scalar", lambda tau=tau, l0=l0: nc.scalar.dma_start(
                out=rows[0:1, 0:2048].rearrange("q (c h i) -> q c h i", c=2, h=16),
                in_=smallfm_d[2, :, tau * TTOK + l0: tau * TTOK + l0 + 128]
                .rearrange("h (c i) -> c h i", c=2).unsqueeze(0)), dma=True,
                deps=[dma_b] + sr)
            br = []
            br.append(p.emit("tensor", lambda: MM(
                P[0:64, 1024:1536], NEG1[0:1, 0:64], rows[0:1, 0:512], start=True, stop=True,
                tile_position=(0, 0)), deps=[r_b, d2p], drain=True))
            br.append(p.emit("tensor", lambda: MM(
                P[0:64, 1536:2048], NEG1[0:1, 0:64], rows[0:1, 512:1024], start=True, stop=True,
                tile_position=(0, 0)), deps=[r_b, d2p]))
            br.append(p.emit("tensor", lambda: MM(
                P[64:128, 1024:1536], NEG1[0:1, 0:64], rows[0:1, 1024:1536], start=True, stop=True,
                tile_position=(0, 64)), deps=[r_b, d2p]))
            br.append(p.emit("tensor", lambda: MM(
                P[64:128, 1536:2048], NEG1[0:1, 0:64], rows[0:1, 1536:2048], start=True, stop=True,
                tile_position=(0, 64)), deps=[r_b, d2p]))
            bm = p.emit("vector", lambda: nc.vector.tensor_mul(
                bmsk[:].rearrange("q (h i) -> q h i", h=16),
                P[:, 1024:2048].rearrange("q (h i) -> q h i", h=16),
                S01[:, None, :].broadcast_to([128, 16, 64])), deps=br)
            btp = p.emit("gpsimd", lambda: nc.gpsimd.tensor_mul(
                bmsk[:, :], bmsk[:, :], es[:, :]), deps=[bm, e_i, mt_])
            bt_h = []
            for rh in range(2):
                bt_h.append(p.emit("vector", lambda rh=rh: nc.vector.tensor_mul(
                    Bt0[:].rearrange("q (khq rh2 par i) -> q khq rh2 par i", khq=4, rh2=2, par=2)
                    [:, :, rh, :, :],
                    akap(rh)[:, :, None, :].broadcast_to([128, 4, 2, 64]),
                    bmsk[:].rearrange("q (khq rh2 par i) -> q khq rh2 par i", khq=4, rh2=2, par=2)
                    [:, :, rh, :, :]), deps=[btp] + ak_mms))
            bt_ = p.emit("vector", lambda: nc.vector.tensor_copy(
                cpool[:, 14:15], cpool[:, 14:15]), deps=bt_h)
            bnt = p.emit("gpsimd", lambda: nc.gpsimd.tensor_mul(
                esp[:].rearrange("q (h j) -> q h j", h=16),
                esp[:].rearrange("q (h j) -> q h j", h=16),
                tmB[:, 64:80][:, :, None].broadcast_to([128, 16, 64])), deps=[e_s, bneg])
            bn_h = []
            for rh in range(2):
                bn_h.append(p.emit("vector", lambda rh=rh: nc.vector.tensor_mul(
                    Bn0[:].rearrange("q (khq rh2 par j) -> q khq rh2 par j", khq=4, rh2=2, par=2)
                    [:, :, rh, :, :],
                    akap(rh)[:, :, None, :].broadcast_to([128, 4, 2, 64]),
                    esp[:].rearrange("q (khq rh2 par j) -> q khq rh2 par j", khq=4, rh2=2, par=2)
                    [:, :, rh, :, :]), deps=[bnt] + ak_mms))
            bn_ = p.emit("vector", lambda: nc.vector.tensor_copy(
                cpool[:, 14:15], cpool[:, 14:15]), deps=bn_h)

            if SUB < 6:
                prev_pair[0] = p.emit("gpsimd", lambda: nc.gpsimd.tensor_copy(
                    cpool[:, 15:16], cpool[:, 15:16]),
                    deps=[bn_, bt_, mt_, khat_ev, rdk, rdv, gamTM, cog, ev_b, ev_g, bgn, bneg] + gcg_evs + [r_g, r_gcc])
                continue
            curR, curBt, curBn = Rhat, Bt0, Bn0
            prod_ev = [rdv, rdk]
            sq_ev = [bt_, bn_]
            if DEBUG:
                pi = tau * 4 + u
                d_bt = p.emit("gpsimd", lambda pi=pi: nc.gpsimd.dma_start(
                    out=dbg["Bt"][pi], in_=Bt0[:, :]), dma=True, deps=[bt_])
                d_rh = p.emit("gpsimd", lambda pi=pi: nc.gpsimd.dma_start(
                    out=dbg["Rhat"][pi], in_=Rhat[:, :]), dma=True, deps=[rdv, rdk])
                d_mt = p.emit("gpsimd", lambda pi=pi: nc.gpsimd.dma_start(
                    out=dbg["mtilT"][pi], in_=mtilT[:, :]), dma=True, deps=[mt_])
                sq_ev = sq_ev + [d_bt]
                prod_ev = prod_ev + [d_rh, d_mt]
            for m in range(6):
                pr = []
                for vh in range(16):
                    for ch in range(2):
                        pr.append(p.emit("tensor", lambda vh=vh, ch=ch, curBt=curBt, curR=curR:
                            MM(
                                P[:, 1024:3072].rearrange("q (h w) -> q h w", h=16)
                                [ch * 64:(ch + 1) * 64, vh, :],
                                curBt[:].rearrange("q (h i) -> q h i", h=16)
                                [ch * 64:(ch + 1) * 64, vh, :],
                                curR[:].rearrange("q (h w) -> q h w", h=16)
                                [ch * 64:(ch + 1) * 64, vh, :],
                                start=True, stop=True,
                                tile_position=(64 * ch, 64 * ch)),
                            deps=prod_ev + sq_ev, drain=(m == 0 and vh == 0 and ch == 0)))
                nxtR = Rm[m % 2]
                pe = p.emit("vector", lambda curR=curR, nxtR=nxtR: nc.vector.tensor_add(
                    nxtR[:, :], P[:, 1024:3072], curR[:, :]), deps=pr)
                prod_ev = [pe]
                curR = nxtR
                if m < 5:
                    sq = []
                    for vh in range(16):
                        for ch in range(2):
                            sq.append(p.emit("tensor", lambda vh=vh, ch=ch, curBt=curBt, curBn=curBn:
                                MM(
                                    P[:, 1024:2048].rearrange("q (h i) -> q h i", h=16)
                                    [ch * 64:(ch + 1) * 64, vh, :],
                                    curBn[:].rearrange("q (h j) -> q h j", h=16)
                                    [ch * 64:(ch + 1) * 64, vh, :],
                                    curBt[:].rearrange("q (h i) -> q h i", h=16)
                                    [ch * 64:(ch + 1) * 64, vh, :],
                                    start=True, stop=True,
                                    tile_position=(64 * ch, 64 * ch)),
                                deps=[pe] + sq_ev))
                            sq.append(p.emit("tensor", lambda vh=vh, ch=ch, curBt=curBt, curBn=curBn:
                                MM(
                                    P[:, 2048:3072].rearrange("q (h j) -> q h j", h=16)
                                    [ch * 64:(ch + 1) * 64, vh, :],
                                    curBt[:].rearrange("q (h i) -> q h i", h=16)
                                    [ch * 64:(ch + 1) * 64, vh, :],
                                    curBn[:].rearrange("q (h j) -> q h j", h=16)
                                    [ch * 64:(ch + 1) * 64, vh, :],
                                    start=True, stop=True,
                                    tile_position=(64 * ch, 64 * ch)),
                                deps=[pe] + sq_ev))
                    nBt, nBn = BtP[m % 2], BnP[m % 2]
                    se1 = p.emit("vector", lambda nBt=nBt: nc.vector.tensor_copy(
                        nBt[:, :], P[:, 1024:2048]), deps=sq)
                    se2 = p.emit("scalar", lambda nBn=nBn: nc.scalar.copy(
                        nBn[:, :], P[:, 2048:3072]), deps=sq)
                    sq_ev = [se1, se2]
                    curBt, curBn = nBt, nBn
            TU = curR
            solve_done = prod_ev + sq_ev
            dbg_pair = []
            if DEBUG:
                pi = tau * 4 + u
                dbg_pair.append(p.emit("gpsimd", lambda TU=TU, pi=pi: nc.gpsimd.dma_start(
                    out=dbg["TU"][pi], in_=TU[:, :]), dma=True, deps=solve_done))

            if SUB < 8:
                prev_pair[0] = p.emit("gpsimd", lambda: nc.gpsimd.tensor_copy(
                    cpool[:, 15:16], cpool[:, 15:16]),
                    deps=solve_done + [bn_, mt_, khat_ev] + [r_g, r_gcc])
                continue
            TUr = TU[:].rearrange("q (h w) -> q h w", h=16)
            mtr = mtilT[:].rearrange("q (h i) -> q h i", h=16)
            khr = Khat[:].rearrange("q (h d) -> q h d", h=16)
            pA = {0: P[:, 3072:3584], 1: P[:, 0:512]}
            okt_all, mt_all, w1_all, seedA = {}, {}, {}, {}
            for ch in range(2):
                okt = [p.emit("tensor", lambda vh=vh, ch=ch: MM(
                    P[:, 2048:2560].rearrange("q (g i) -> q g i", g=8)
                    [vh_map(vh)[0] * 64:(vh_map(vh)[0] + 1) * 64, vh_map(vh)[1], :],
                    TUr[ch * 64:(ch + 1) * 64, vh, 64:128],
                    mtr[ch * 64:(ch + 1) * 64, vh, :],
                    start=True, stop=True,
                    tile_position=(64 * ch, 64 * vh_map(vh)[0])),
                    deps=solve_done + [mt_] + ([okt_all[0]] if ch == 1 else []))
                    for vh in range(16)]
                ev_okt = p.emit("vector", lambda ch=ch: nc.vector.tensor_copy(
                    OKTn[:, tc(ch * 512)], P[:, 2048:2560]), deps=okt)
                okt_all[ch] = ev_okt
                w2t = [p.emit("tensor", lambda vh=vh, ch=ch: MM(
                    P[:, 2560:3072].rearrange("q (g d) -> q g d", g=8)
                    [vh_map(vh)[0] * 64:(vh_map(vh)[0] + 1) * 64, vh_map(vh)[1], :],
                    TUr[ch * 64:(ch + 1) * 64, vh, 64:128],
                    khr[ch * 64:(ch + 1) * 64, vh, :],
                    start=True, stop=True,
                    tile_position=(64 * ch, 64 * vh_map(vh)[0])),
                    deps=solve_done + [khat_ev] + ([mt_all[0]] if ch == 1 else []))
                    for vh in range(16)]
                dgb_h = [p.emit("vector", lambda ch=ch, rh=rh: nc.vector.tensor_mul(
                    dgc[rh * 64:(rh + 1) * 64, :].rearrange("q (a d i) -> q a d i", a=4, d=2),
                    IST2[rh * 64:(rh + 1) * 64, :][:, None, None, :].broadcast_to([64, 4, 2, 64]),
                    tmB[rh * 64:(rh + 1) * 64, 80 + 16 * ch:96 + 16 * ch]
                    .rearrange("q (a x) -> q a x", a=4)[:, :, 2 * rh:2 * rh + 2][:, :, :, None]
                    .broadcast_to([64, 4, 2, 64])),
                    deps=[gcg_evs[ch]] + ([mt_all[0]] if ch == 1 else []))
                    for rh in range(2)]
                dgb = p.emit("vector", lambda: nc.vector.tensor_copy(
                    cpool[:, 14:15], cpool[:, 14:15]), deps=dgb_h)
                mt_ev = p.emit("vector", lambda ch=ch: nc.vector.tensor_add(
                    Mts[:, tc(ch * 512)], dgc[:, :], P[:, 2560:3072]), deps=w2t + [dgb])
                mt_all[ch] = mt_ev
                w1 = [p.emit("tensor", lambda vh=vh, ch=ch: MM(
                    P[:, 1024 + ch * 512:1536 + ch * 512].rearrange("q (g d) -> q g d", g=8)
                    [vh_map(vh)[0] * 64:(vh_map(vh)[0] + 1) * 64, vh_map(vh)[1], :],
                    khr[ch * 64:(ch + 1) * 64, vh, :],
                    TUr[ch * 64:(ch + 1) * 64, vh, 0:64],
                    start=(vh in (0, 2)), stop=False,
                    tile_position=(64 * ch, 64 * vh_map(vh)[0])),
                    deps=solve_done + [khat_ev, ev_okt])
                    for vh in range(16)]
                w1_all[ch] = w1
                seedA[ch] = [p.emit("tensor", lambda vh=vh, ch=ch: MM(
                    pA[ch].rearrange("q (g i) -> q g i", g=8)
                    [vh_map(vh)[3] * 64:(vh_map(vh)[3] + 1) * 64, vh // 2, :],
                    TUr[ch * 64:(ch + 1) * 64, vh, 0:64],
                    mtr[ch * 64:(ch + 1) * 64, vh, :],
                    start=(vh in (0, 1)), stop=False,
                    tile_position=(64 * ch, 64 * vh_map(vh)[3])),
                    deps=solve_done + [mt_, ev_b, ev_g, gamTM, cog] + gcg_evs)
                    for vh in range(16)]
            if DEBUG:
                dbg_pair.append(p.emit("gpsimd", lambda pi=tau * 4 + u: nc.gpsimd.dma_start(
                    out=dbg["Mts"][pi], in_=Mts[:, :]), dma=True, deps=[mt_all[0], mt_all[1]]))
                dbg_pair.append(p.emit("gpsimd", lambda pi=tau * 4 + u: nc.gpsimd.dma_start(
                    out=dbg["okt"][pi], in_=OKTn[:, :]), dma=True, deps=[okt_all[0], okt_all[1]]))

            if SUB < 10:
                prev_pair[0] = p.emit("gpsimd", lambda: nc.gpsimd.tensor_copy(
                    cpool[:, 15:16], cpool[:, 15:16]),
                    deps=[mt_all[0], mt_all[1], okt_all[0], okt_all[1]] + w1_all[0] + w1_all[1] + seedA[0] + seedA[1] + [r_g, r_gcc])
                continue
            prev_epln = [None]
            chunk_tail = []
            for ch in range(2):
                cglob = cg0 + ch
                lch = l0 + ch * 64
                rge = p.emit("scalar", lambda tau=tau, lch=lch: nc.scalar.dma_start(
                    out=rows[0:1, 0:512].rearrange("q (a i) -> q a i", a=8),
                    in_=smallfm_d[1, 0:16:2, tau * TTOK + lch: tau * TTOK + lch + 64].unsqueeze(0)),
                    dma=True, deps=[dma_gm] + br)
                rgo = p.emit("scalar", lambda tau=tau, lch=lch: nc.scalar.dma_start(
                    out=rows[0:1, 512:1024].rearrange("q (a i) -> q a i", a=8),
                    in_=smallfm_d[1, 1:16:2, tau * TTOK + lch: tau * TTOK + lch + 64].unsqueeze(0)),
                    dma=True, deps=[dma_gm] + br)
                kg1 = p.emit("tensor", lambda: MM(
                    P[0:64, 512:1024], ONES[0:1, 0:64], rows[0:1, 0:512], start=True, stop=True,
                    tile_position=(0, 0)),
                    deps=[rge, prev_epln[0], mt_] + (aqt_mms if ch == 0 else []), drain=True)
                kg2 = p.emit("tensor", lambda: MM(
                    P[64:128, 512:1024], ONES[0:1, 0:64], rows[0:1, 512:1024], start=True, stop=True,
                    tile_position=(0, 64)), deps=[rgo, prev_epln[0]])
                gev = p.emit("vector", lambda: nc.vector.tensor_copy(
                    Gevac[:, :], P[:, 512:1024]), deps=[kg1, kg2, mt_])
                qs_mms, oks_mms, sp_mms = [], [], []
                oks_rh0_last = [None]
                vh_order = [v for v in range(16) if vh_map(v)[0] == 0] + \
                           [v for v in range(16) if vh_map(v)[0] == 1]
                for vh in vh_order:
                    rh, vhg, fc, par = vh_map(vh)
                    qs_mms.append(p.emit("tensor",
                        lambda vh=vh, rh=rh, vhg=vhg, fc=fc, par=par, lch=lch:
                        MM(
                            P[:, (512 if rh == 0 else 2048) + (vh // 4) * 64:
                              (512 if rh == 0 else 2048) + (vh // 4) * 64 + 64]
                            [par * 64:(par + 1) * 64, :],
                            Ssb[rh * 64:(rh + 1) * 64, tc(vhg * 64, 64)],
                            qsl[64 * rh:64 * (rh + 1), fc, lch:lch + 64],
                            start=True, stop=True,
                            tile_position=(64 * rh, 64 * par)),
                        deps=[gev, prev_state[0], okt_all[1], mt_all[1]],
                        drain=(vh == vh_order[0])))
                    oks_mms.append(p.emit("tensor",
                        lambda vh=vh, rh=rh, vhg=vhg, par=par, ch=ch:
                        MM(
                            pA[ch].rearrange("q (g i) -> q g i", g=8)
                            [par * 64:(par + 1) * 64, vh // 2, :],
                            Ssb[rh * 64:(rh + 1) * 64, tc(vhg * 64, 64)],
                            OKTn[:, tc(ch * 512)].rearrange("q (g i) -> q g i", g=8)
                            [rh * 64:(rh + 1) * 64, vhg, :],
                            start=False, stop=(vh >= 14), tile_position=(64 * rh, 64 * par)),
                        deps=[okt_all[ch], prev_state[0]] + seedA[ch],
                        hard=([oks_rh0_last[0]] if rh == 1 else [])))
                    if rh == 0:
                        oks_rh0_last[0] = oks_mms[-1]
                    sp_mms.append(p.emit("tensor",
                        lambda vh=vh, rh=rh, vhg=vhg, ch=ch:
                        MM(
                            P[:, 1024 + ch * 512:1536 + ch * 512]
                            .rearrange("q (g d) -> q g d", g=8)
                            [rh * 64:(rh + 1) * 64, vhg, :],
                            Mts[:, tc(ch * 512)].rearrange("q (g d) -> q g d", g=8)
                            [rh * 64:(rh + 1) * 64, vhg, :],
                            Ssb[rh * 64:(rh + 1) * 64, tc(vhg * 64, 64)],
                            start=False, stop=(vh in (13, 15)), tile_position=(64 * rh, 64 * rh)),
                        deps=[mt_all[ch], prev_state[0]] + w1_all[ch]))
                sev = p.emit("scalar", lambda ch=ch: nc.scalar.copy(
                    Ssb[:, :], P[:, 1024 + ch * 512:1536 + ch * 512]),
                    deps=qs_mms + oks_mms + sp_mms)
                prev_state[0] = sev
                if DEBUG:
                    prev_state[0] = p.emit("gpsimd", lambda cglob=cglob: nc.gpsimd.dma_start(
                        out=dbg["S"][cglob], in_=Ssb[:, :]), dma=True, deps=[sev])
                ep_t_h = []
                for rh in range(2):
                    ep_t_h.append(p.emit("vector", lambda rh=rh: nc.vector.tensor_mul(
                        ep1[:].rearrange("q (pq rh2 i) -> q pq rh2 i", pq=4, rh2=2)[:, :, rh, :],
                        P[:, (512 if rh == 0 else 2048): (512 if rh == 0 else 2048) + 256]
                        .rearrange("q (pq i) -> q pq i", pq=4),
                        Gevac[:].rearrange("q (pq rh2 i) -> q pq rh2 i", pq=4, rh2=2)[:, :, rh, :]),
                        deps=qs_mms + [gev]))
                ep_t = p.emit("vector", lambda: nc.vector.tensor_copy(
                    cpool[:, 14:15], cpool[:, 14:15]), deps=ep_t_h)
                ep_t2 = p.emit("vector", lambda ch=ch: nc.vector.tensor_add(
                    ep2[:, :], pA[ch], ep1[:, :]), deps=[ep_t] + oks_mms + seedA[ch])
                ep_sq = p.emit("scalar", lambda: nc.scalar.activation(
                    o2s[:, :], ep2[:, :], AF.Square), deps=[ep_t2])
                ep_mm = p.emit("tensor", lambda: MM(
                    P[:, 512:1024], BD64b, o2s[:, :], start=True, stop=True),
                    deps=[ep_sq] + qs_mms, drain=True)
                ep_ln = p.emit("scalar", lambda: nc.scalar.activation(
                    rinvs[:, :], P[:, 512:1024], AF.Ln, scale=1.0 / 64.0, bias=cp(1e-6)),
                    deps=[ep_mm])
                ep_ex = p.emit("scalar", lambda: nc.scalar.activation(
                    rinvs[:, :], rinvs[:, :], AF.Exp, scale=-0.5), deps=[ep_ln])
                prev_epln[0] = ep_ex
                ep_t3 = p.emit("vector", lambda: nc.vector.tensor_mul(
                    ep1[:, :], ep2[:, :], rinvs[:, :]), deps=[ep_ex, ep_t2])
                ep_t4 = p.emit("vector", lambda lch=lch: nc.vector.scalar_tensor_tensor(
                    ogat[:].rearrange("q (cc pg i) -> q cc pg i", cc=8, pg=8)[:, lch // 64, :, :],
                    ep1[:].rearrange("q (pg i) -> q pg i", pg=8), pp[:, 0:1],
                    zsilu[:].rearrange("q (a t) -> q a t", a=8)[:, :, lch:lch + 64],
                    ALU.mult, ALU.mult), deps=[ep_t3, ld_pp] + zevs)
                chunk_tail.append(ep_t4)
                chunk_tail.append(sev)
            prev_pair[0] = p.emit("gpsimd", lambda: nc.gpsimd.tensor_copy(
                cpool[:, 15:16], cpool[:, 15:16]),
                deps=chunk_tail + [mt_all[1], okt_all[1], khat_ev, rdk, bn_] + dbg_pair)

        if PHASE < 3 or not WOUT:
            break
        og_dep = [prev_pair[0]]
        wo_evs = []
        prev_w = None
        for mc in range(8):
            mms = []
            for kc in range(8):
                mms.append(p.emit("tensor", lambda mc=mc, kc=kc: MM(
                    P[:, 1024:1536],
                    Wout[:].rearrange("q (kc m) -> q kc m", kc=8)[:, kc, tc(mc * 128, 128)],
                    ogat[:].rearrange("q (cc w) -> q cc w", cc=8)[:, :, tc(kc * 64, 64)],
                    start=(kc == 0), stop=(kc == 7)), deps=og_dep + [prev_w]))
            prev_w = p.emit("scalar", lambda mc=mc: nc.scalar.copy(
                aob[:, tc(mc * 512)], P[:, 1024:1536]), deps=[mms[-1]])
            wo_evs.append(prev_w)
        mm_a1 = [p.emit("tensor", lambda mc=mc: MM(
            P[:, 0:512], ONESb, aob[:, tc(mc * 512)], start=(mc == 0), stop=(mc == 7)),
            deps=[wo_evs[mc]] + og_dep) for mc in range(8)]
        asqs = [p.emit("scalar", lambda mc=mc: nc.scalar.activation(
            vsilu[:, tc(mc * 512)], aob[:, tc(mc * 512)], AF.Square),
            deps=[wo_evs[mc]] + og_dep) for mc in range(8)]
        mm_a2 = [p.emit("tensor", lambda mc=mc: MM(
            P[:, 512:1024], ONESb, vsilu[:, tc(mc * 512)], start=(mc == 0), stop=(mc == 7)),
            deps=[asqs[mc]]) for mc in range(8)]
        amu = p.emit("vector", lambda: nc.vector.tensor_scalar_mul(
            stat[:, 0:512], P[:, 0:512], 1.0 / 1024.0), deps=[mm_a1[-1]])
        amu2 = p.emit("scalar", lambda: nc.scalar.activation(
            stat[:, 1024:1536], stat[:, 0:512], AF.Square), deps=[amu])
        as2 = p.emit("vector", lambda: nc.vector.tensor_scalar_mul(
            stat[:, 512:1024], P[:, 512:1024], 1.0 / 1024.0), deps=[mm_a2[-1]])
        avar = p.emit("vector", lambda: nc.vector.tensor_sub(
            stat[:, 512:1024], stat[:, 512:1024], stat[:, 1024:1536]), deps=[as2, amu2])
        alnv = p.emit("scalar", lambda: nc.scalar.activation(
            stat[:, 1024:1536], stat[:, 512:1024], AF.Ln, bias=cp(1e-5)), deps=[avar])
        arin = p.emit("scalar", lambda: nc.scalar.activation(
            stat[:, 512:1024], stat[:, 1024:1536], AF.Exp, scale=-0.5), deps=[alnv])
        tile_end = []
        for dc in range(8):
            s1_ = p.emit("vector", lambda dc=dc: nc.vector.tensor_sub(
                convt[:, 0:512], aob[:, tc(dc * 512)], stat[:, 0:512]), deps=[amu, arin, wo_evs[dc]])
            s2_ = p.emit("vector", lambda dc=dc: nc.vector.tensor_mul(
                convt[:, 512:1024], convt[:, 0:512], stat[:, 512:1024]), deps=[s1_])
            s3_ = p.emit("vector", lambda dc=dc: nc.vector.tensor_add(
                xtile[:, tc(dc * 512)], xtile[:, tc(dc * 512)], convt[:, 512:1024]), deps=[s2_])
            wdma = p.emit("sync", lambda dc=dc, tau=tau: nc.sync.dma_start(
                out=ag_in.rearrange("(dc q) t -> q dc t", q=128)[:, dc, tc(tau * TTOK)],
                in_=xtile[:, tc(dc * 512)]), dma=True, deps=[s3_])
            ag_writes.append(wdma)
            tile_end.append(wdma)
            if DEBUG:
                tile_end.append(p.emit("sync", lambda dc=dc, tau=tau: nc.sync.dma_start(
                    out=dbg["xmid"][tau].rearrange("q (dc t) -> q dc t", dc=8)[:, dc, :],
                    in_=xtile[:, tc(dc * 512)]), dma=True, deps=[s3_]))
        if DEBUG:
            p.emit("gpsimd", lambda tau=tau: nc.gpsimd.dma_start(
                out=dbg["ogated"][tau], in_=ogat[:, :]), dma=True, deps=og_dep)
        prev_tile_done = p.barrier() + tile_end

    if PHASE < 5:
        from contextlib import ExitStack
        with ExitStack() as _st:
            sem_handles = {}
            for e in ENGINES:
                sem_handles[e] = _st.enter_context(nc.semaphore("s_" + e))
                for ln in range(NDMALANES):
                    sem_handles[(e, ln)] = _st.enter_context(nc.semaphore(f"d_{e}{ln}"))
            block = _st.enter_context(nc.Block())
            fdeps = [p.lastop[e] for e in ENGINES if p.lastop[e] is not None]
            p.finalize(block, sem_handles, final_deps=fdeps)
        return nc
    cc = p.emit("gpsimd", lambda: nc.gpsimd.collective_compute(
        "AllGather", ALU.bypass, replica_groups=[list(range(NCORE))],
        ins=[ag_in[:, :]], outs=[ag_out[:, :]]), deps=ag_writes + p.barrier(), inc=1)

    post_attn = p.barrier()
    wffn_lds = [p.emit("gpsimd", lambda wi=wi: nc.gpsimd.dma_start(
        out=Wffn[:].rearrange("q (w k) -> q w k", w=8)[:, wi, :].rearrange("q (kc m) -> q kc m", kc=8),
        in_=wffn[:, wi]), dma=True, deps=post_attn) for wi in range(8)]
    ld_bf = p.emit("sync", lambda: nc.sync.dma_start(
        out=bffn_s[:, :].rearrange("q (k m) -> q k m", k=5), in_=bffn), dma=True, deps=post_attn)

    offh = {}
    offreg = [None]

    def getoff():
        if offreg[0] is None:
            offreg[0] = nc.alloc_register(mybir.EngineType.Pool, "offreg")
        nc.gpsimd.reg_load(offreg[0], soff[0:1, 0:1])
        return nc.gpsimd.snap(offreg[0], min_val=0, max_val=4096 - 512)

    off_op = p.emit("gpsimd", lambda: nc.gpsimd.dma_start(out=soff[:, :], in_=coreoff),
                    dma=True, deps=[ld_off] + post_attn)

    Wr = Wffn[:].rearrange("q (w kc m) -> q w kc m", w=8, kc=8)
    WF, UF, WI, UI, WC, UC, WO, UO = range(8)

    cinit = []
    for mc in range(8):
        cinit.append(p.emit("vector", lambda mc=mc: nc.vector.tensor_scalar(
            cstate[:, tc(mc * 512)], cpool[:, 0:1].broadcast_to([128, 512]), 0.0,
            bffn_s[:, 32 + mc:33 + mc], ALU.mult, ALU.add), deps=[ld_bf] + post_attn))
        cinit.append(p.emit("scalar", lambda mc=mc: nc.scalar.copy(
            cbf[:, tc(mc * 512)], cstate[:, tc(mc * 512)]), deps=[cinit[-1]]))

    ln2_w = {}
    prev_step = [None]
    for t in range(8):
        xl0 = p.emit("gpsimd", lambda t=t: nc.gpsimd.dma_start(
            out=xmt[:].rearrange("q (dc w) -> q dc w", dc=8),
            in_=ag_out[t * 1024:(t + 1) * 1024, :][:, bass.ds(getoff(), 512)]
            .rearrange("(dc q) w -> q dc w", q=128)),
            dma=True, deps=[off_op, cc] + ([prev_step[0]] if prev_step[0] else []))
        xl = [xl0] * 8
        cst2 = [p.emit("vector", lambda dc=dc: nc.vector.tensor_copy(
            ln2t[:, tc(dc * 512)], xmt[:, tc(dc * 512)]), deps=[xl[dc]]) for dc in range(8)]
        f_s1 = [p.emit("tensor", lambda dc=dc: MM(
            P[:, 0:512], ONESb, ln2t[:, tc(dc * 512)], start=(dc == 0), stop=(dc == 7)),
            deps=[cst2[dc]] + ([prev_step[0]] if prev_step[0] else post_attn)) for dc in range(8)]
        sq2, f_s2 = [], []
        for dc in range(8):
            sq2.append(p.emit("scalar", lambda dc=dc: nc.scalar.activation(
                hsq[:, 0:512], ln2t[:, tc(dc * 512)], AF.Square),
                deps=[cst2[dc]] + ([f_s2[dc - 1]] if dc else [])))
            f_s2.append(p.emit("tensor", lambda dc=dc: MM(
                P[:, 512:1024], ONESb, hsq[:, 0:512], start=(dc == 0), stop=(dc == 7)),
                deps=[sq2[dc]]))
        fmu = p.emit("vector", lambda: nc.vector.tensor_scalar_mul(
            statf[:, 0:512], P[:, 0:512], 1.0 / 1024.0), deps=[f_s1[-1]])
        fmu2 = p.emit("scalar", lambda: nc.scalar.activation(
            statf[:, 1024:1536], statf[:, 0:512], AF.Square), deps=[fmu])
        fs2m = p.emit("vector", lambda: nc.vector.tensor_scalar_mul(
            statf[:, 512:1024], P[:, 512:1024], 1.0 / 1024.0), deps=[f_s2[-1]])
        fvar = p.emit("vector", lambda: nc.vector.tensor_sub(
            statf[:, 512:1024], statf[:, 512:1024], statf[:, 1024:1536]), deps=[fs2m, fmu2])
        flnv = p.emit("scalar", lambda: nc.scalar.activation(
            statf[:, 1024:1536], statf[:, 512:1024], AF.Ln, bias=cp(1e-5)), deps=[fvar])
        frin = p.emit("scalar", lambda: nc.scalar.activation(
            statf[:, 512:1024], statf[:, 1024:1536], AF.Exp, scale=-0.5), deps=[flnv])
        l2o = []
        for dc in range(8):
            aa = p.emit("vector", lambda dc=dc: nc.vector.tensor_sub(
                outt[:, 0:512], xmt[:, tc(dc * 512)], statf[:, 0:512]), deps=[fmu, frin])
            bb = p.emit("vector", lambda dc=dc: nc.vector.tensor_mul(
                ln2t[:, tc(dc * 512)], outt[:, 0:512], statf[:, 512:1024]), deps=[aa])
            l2o.append(bb)
        ln2_w[t] = p.emit("sync", lambda t=t: nc.sync.dma_start(
            out=ln2_d[t].rearrange("dc q tt -> q dc tt"), in_=ln2t[:].rearrange("q (dc tt) -> q dc tt", dc=8)),
            dma=True, deps=l2o)
        step_tail = []
        for mc in range(8):
            gate_psum = {0: P[:, 1024:1536], 1: P[:, 1536:2048], 2: P[:, 2048:2560]}
            gmms = {0: [], 1: [], 2: []}
            for gi, (wx, ux) in enumerate(((WF, UF), (WI, UI), (WC, UC))):
                for kc in range(8):
                    gmms[gi].append(p.emit("tensor", lambda gi=gi, wx=wx, mc=mc, kc=kc:
                        MM(
                            gate_psum[gi], Wr[:, wx, kc, tc(mc * 128, 128)],
                            ln2t[:, tc(kc * 512)], start=(kc == 0), stop=False),
                        deps=[l2o[kc], wffn_lds[wx]] +
                        ([prev_step[0]] if prev_step[0] and mc == 0 else []) +
                        ([step_tail[-1]] if step_tail else [])))
                for kc in range(8):
                    gmms[gi].append(p.emit("tensor", lambda gi=gi, ux=ux, mc=mc, kc=kc:
                        MM(
                            gate_psum[gi], Wr[:, ux, kc, tc(mc * 128, 128)],
                            cbf[:, tc(kc * 512)], start=False, stop=(kc == 7)),
                        deps=[wffn_lds[ux], cinit[2 * kc + 1] if t == 0 else None]))
            fev = p.emit("scalar", lambda mc=mc: nc.scalar.activation(
                fgi[:, 0:512], P[:, 1024:1536], AF.Sigmoid, bias=bffn_s[:, 0 * 8 + mc:0 * 8 + mc + 1]),
                deps=[gmms[0][-1], ld_bf])
            iev = p.emit("scalar", lambda mc=mc: nc.scalar.activation(
                fgi[:, 512:1024], P[:, 1536:2048], AF.Sigmoid, bias=bffn_s[:, 1 * 8 + mc:1 * 8 + mc + 1]),
                deps=[gmms[1][-1], ld_bf])
            gev_ = p.emit("scalar", lambda mc=mc: nc.scalar.activation(
                fgi[:, 1024:1536], P[:, 2048:2560], AF.Tanh, bias=bffn_s[:, 2 * 8 + mc:2 * 8 + mc + 1]),
                deps=[gmms[2][-1], ld_bf])
            u1 = p.emit("vector", lambda mc=mc: nc.vector.tensor_mul(
                cstate[:, tc(mc * 512)], cstate[:, tc(mc * 512)], fgi[:, 0:512]), deps=[fev])
            u2 = p.emit("vector", lambda mc=mc: nc.vector.tensor_mul(
                fgi[:, 1024:1536], fgi[:, 512:1024], fgi[:, 1024:1536]), deps=[iev, gev_])
            u3 = p.emit("vector", lambda mc=mc: nc.vector.tensor_add(
                cstate[:, tc(mc * 512)], cstate[:, tc(mc * 512)], fgi[:, 1024:1536]),
                deps=[u1, u2])
            step_tail.append(u3)
        cb_up = [p.emit("scalar", lambda mc=mc: nc.scalar.copy(
            cbf[:, tc(mc * 512)], cstate[:, tc(mc * 512)]), deps=step_tail) for mc in range(8)]
        prev_step[0] = p.emit("gpsimd", lambda: nc.gpsimd.tensor_copy(
            cpool[:, 15:16], cpool[:, 15:16]), deps=cb_up + [ln2_w[t]])

    final_outs = []
    prev_h = [prev_step[0]]
    for t in range(8):
        lrd = [p.emit("sync", lambda t=t, dc=dc: nc.sync.dma_start(
            out=ln2t[:, tc(dc * 512)], in_=ln2_d[t, dc]), dma=True,
            deps=[ln2_w[t], prev_h[0]]) for dc in range(8)]
        xrd0 = p.emit("gpsimd", lambda t=t: nc.gpsimd.dma_start(
            out=xmt[:].rearrange("q (dc w) -> q dc w", dc=8),
            in_=ag_out[t * 1024:(t + 1) * 1024, :][:, bass.ds(getoff(), 512)]
            .rearrange("(dc q) w -> q dc w", q=128)), dma=True,
            deps=[off_op, cc, prev_h[0]])
        xrd = [xrd0] * 8
        hevs = []
        prev_hm = None
        for mc in range(8):
            mms = []
            for kc in range(8):
                mms.append(p.emit("tensor", lambda mc=mc, kc=kc: MM(
                    P[:, 1024:1536], Wr[:, WO, kc, tc(mc * 128, 128)],
                    ln2t[:, tc(kc * 512)], start=(kc == 0), stop=False),
                    deps=[lrd[kc], wffn_lds[WO], prev_hm] + ([prev_h[0]] if mc == 0 else [])))
            for kc in range(8):
                mms.append(p.emit("tensor", lambda mc=mc, kc=kc: MM(
                    P[:, 1024:1536], Wr[:, UO, kc, tc(mc * 128, 128)],
                    cbf[:, tc(kc * 512)], start=False, stop=(kc == 7)),
                    deps=[wffn_lds[UO]]))
            prev_hm = p.emit("scalar", lambda mc=mc: nc.scalar.activation(
                hbt[:, tc(mc * 512)], P[:, 1024:1536], AF.Tanh,
                bias=bffn_s[:, 3 * 8 + mc:3 * 8 + mc + 1]), deps=[mms[-1], ld_bf])
            hevs.append(prev_hm)
        h_s1 = [p.emit("tensor", lambda mc=mc: MM(
            P[:, 0:512], ONESb, hbt[:, tc(mc * 512)], start=(mc == 0), stop=(mc == 7)),
            deps=[hevs[mc]]) for mc in range(8)]
        hsqs, h_s2 = [], []
        for mc in range(8):
            hsqs.append(p.emit("scalar", lambda mc=mc: nc.scalar.activation(
                hsq[:, 0:512], hbt[:, tc(mc * 512)], AF.Square),
                deps=[hevs[mc]] + ([h_s2[mc - 1]] if mc else [])))
            h_s2.append(p.emit("tensor", lambda mc=mc: MM(
                P[:, 512:1024], ONESb, hsq[:, 0:512], start=(mc == 0), stop=(mc == 7)),
                deps=[hsqs[mc]]))
        hmu = p.emit("vector", lambda: nc.vector.tensor_scalar_mul(
            statf[:, 0:512], P[:, 0:512], 1.0 / 1024.0), deps=[h_s1[-1]])
        hmu2 = p.emit("scalar", lambda: nc.scalar.activation(
            statf[:, 1024:1536], statf[:, 0:512], AF.Square), deps=[hmu])
        hs2m = p.emit("vector", lambda: nc.vector.tensor_scalar_mul(
            statf[:, 512:1024], P[:, 512:1024], 1.0 / 1024.0), deps=[h_s2[-1]])
        hvar = p.emit("vector", lambda: nc.vector.tensor_sub(
            statf[:, 512:1024], statf[:, 512:1024], statf[:, 1024:1536]), deps=[hs2m, hmu2])
        hlnv = p.emit("scalar", lambda: nc.scalar.activation(
            statf[:, 1024:1536], statf[:, 512:1024], AF.Ln, bias=cp(1e-5)), deps=[hvar])
        hrin = p.emit("scalar", lambda: nc.scalar.activation(
            statf[:, 512:1024], statf[:, 1024:1536], AF.Exp, scale=-0.5), deps=[hlnv])
        prev_od = None
        for dc in range(8):
            q1 = p.emit("vector", lambda dc=dc: nc.vector.tensor_sub(
                outt[:, 0:512], hbt[:, tc(dc * 512)], statf[:, 0:512]),
                deps=[hmu, hrin, prev_od])
            q2 = p.emit("vector", lambda dc=dc: nc.vector.tensor_mul(
                outt[:, 512:1024], outt[:, 0:512], statf[:, 512:1024]), deps=[q1])
            q3 = p.emit("vector", lambda dc=dc: nc.vector.tensor_add(
                outt[:, 0:512], outt[:, 512:1024], xmt[:, tc(dc * 512)]),
                deps=[q2, xrd[dc]])
            od = p.emit("sync", lambda t=t, dc=dc: nc.sync.dma_start(
                out=out_fm[t, dc], in_=outt[:, 0:512]), dma=True, deps=[q3])
            prev_od = od
            final_outs.append(od)
        prev_h[0] = p.emit("gpsimd", lambda: nc.gpsimd.tensor_copy(
            cpool[:, 15:16], cpool[:, 15:16]), deps=final_outs[-8:] + hevs)

    from contextlib import ExitStack
    with ExitStack() as _st:
        sem_handles = {}
        for e in ENGINES:
            sem_handles[e] = _st.enter_context(nc.semaphore("s_" + e))
            for ln in range(NDMALANES):
                sem_handles[(e, ln)] = _st.enter_context(nc.semaphore(f"d_{e}{ln}"))
        block = _st.enter_context(nc.Block())
        p.finalize(block, sem_handles, final_deps=final_outs)
    return nc



def prep_inputs(inputs):
    x = np.asarray(inputs["L0_attn_hidden_states"], dtype=np.float32)
    wqkv = np.asarray(inputs["L0_attn_W_qkv"], dtype=np.float32)
    wz = np.asarray(inputs["L0_attn_W_z"], dtype=np.float32)
    wb = np.asarray(inputs["L0_attn_W_b"], dtype=np.float32)
    wa = np.asarray(inputs["L0_attn_W_a"], dtype=np.float32)
    dtb = np.asarray(inputs["L0_attn_dt_bias"], dtype=np.float32)
    alog = np.asarray(inputs["L0_attn_A_log"], dtype=np.float32)
    cw = np.asarray(inputs["L0_attn_conv_w"], dtype=np.float32)
    nw = np.asarray(inputs["L0_attn_norm_w"], dtype=np.float32)
    wo_a = np.asarray(inputs["L0_attn_W_out"], dtype=np.float32)

    wba = np.zeros((1024, 64), dtype=np.float32)
    wba[:, 0:16] = wb
    wba[:, 32:48] = wa

    def lhsT(w):
        return np.ascontiguousarray(w.reshape(w.shape[0] // 128, 128, w.shape[1]).transpose(1, 0, 2))

    ppack = np.zeros((128, 4), dtype=np.float32)
    ppack[:, 0] = np.tile(nw, 2)
    ppack[32:48, 1] = alog
    ppack[32:48, 2] = dtb

    ffn_ws = []
    for k in ("wf", "uf", "wi", "ui", "wc", "uc", "wo", "uo"):
        w = np.asarray(inputs["L0_ffn_" + k], dtype=np.float32)
        ffn_ws.append(lhsT(np.ascontiguousarray(w.T)))
    wffn = np.ascontiguousarray(np.stack(ffn_ws, axis=1))

    bffn = np.zeros((128, 5, 8), dtype=np.float32)
    for ki, k in enumerate(("bf", "bi", "bc", "bo", "in_c")):
        b = np.asarray(inputs["L0_ffn_" + k], dtype=np.float32)
        bffn[:, ki, :] = b.reshape(8, 128).T

    cstd = make_consts()
    base = dict(
        wqkv=lhsT(wqkv), wz=lhsT(wz), wba=lhsT(wba), wout=lhsT(wo_a),
        convw=np.ascontiguousarray(cw.reshape(16, 128, 4).transpose(1, 0, 2)),
        ppack=ppack, cstd=cstd, wffn=wffn, bffn=bffn,
    )
    in_maps = []
    for i in range(NCORE):
        m = dict(base)
        m["xw"] = np.ascontiguousarray(x[i].T)
        m["coreoff"] = np.array([[512 * i]], dtype=np.int32)
        in_maps.append(m)
    return in_maps


_NC_CACHE = {}


def kernel(**inputs):
    if "nc" not in _NC_CACHE:
        _NC_CACHE["nc"] = build_program()
    nc = _NC_CACHE["nc"]
    in_maps = prep_inputs(inputs)
    res = run_bass_kernel_spmd(nc, in_maps, core_ids=list(range(NCORE)))
    outs = []
    for i in range(NCORE):
        o = np.asarray(res.results[i]["out_fm"])
        o = o.reshape(8, 1024, 512).transpose(0, 2, 1)
        outs.append(o)
    full = np.concatenate(outs, axis=1)
    if "res" in _NC_CACHE:
        pass
    kernel.last_results = res
    return np.ascontiguousarray(full, dtype=np.float32)


if __name__ == "__main__":
    print("module")
